# version 01
# baseline (speedup 1.0000x reference)
import numpy as np
import concourse.bass as bass
import concourse.mybir as mybir
from concourse.bass_utils import run_bass_kernel_spmd

F32 = mybir.dt.float32
AF = mybir.ActivationFunctionType
ALU = mybir.AluOpType

B, T, D = 4, 4096, 1024
HQ, HKV, HD = 16, 4, 64
KW = 4
DAMP = 0.25
EPS_GLA = 1e-6
EPS_LN = 1e-5
C = 64
TC = 128
PRE_TC = 15
MAIN_TC = 17
NQ = 4
FCH = 8
QKV_CH = 12
AUXF = 96

ENGINES = ("sync", "tensor", "vector", "scalar", "gpsimd")
SIM = None


def simulate_schedule():
    items = SIM["items"]
    ptr = {e: 0 for e in ENGINES}
    cnt = {e: 0 for e in ENGINES}
    dcnt = {"dsync": 0, "dgps": 0}

    def ok(waits):
        for s, c in waits:
            if s in dcnt:
                if dcnt[s] < c:
                    return False
            elif cnt[s] < c:
                return False
        return True

    progress = True
    while progress:
        progress = False
        for e in ENGINES:
            while ptr[e] < len(items[e]):
                kind, waits, label, writes = items[e][ptr[e]]
                if not ok(waits):
                    break
                if kind == "grp":
                    cnt[e] += 1
                else:
                    dcnt["dsync" if e == "sync" else "dgps"] += 1
                ptr[e] += 1
                progress = True
    stuck = [(e, ptr[e], len(items[e])) for e in ENGINES
             if ptr[e] < len(items[e])]
    if stuck:
        print("DEADLOCK; engine positions:", stuck)
        print("sem counts:", cnt, dcnt)
        for e, p, n in stuck:
            kind, waits, label, writes = items[e][p]
            print(f"  {e}[{p}/{n}] {kind} '{label}' writes={writes} "
                  f"blocked on {[(s, c) for s, c in waits if (dcnt[s] if s in dcnt else cnt[s]) < c]}")
        return False
    print("schedule OK: no deadlock")
    return True


class R:

    def __init__(self, tens, col0, ncols, nrows=128):
        self._a = tens[:, :]
        self.ps = self._a.ap[0][0]
        self.col0 = col0
        self.ncols = ncols
        self.nrows = nrows

    def __getitem__(self, idx):
        rs, cs = idx
        r0 = rs.start or 0
        r1 = rs.stop if rs.stop is not None else self.nrows
        c0 = cs.start or 0
        c1 = cs.stop if cs.stop is not None else self.ncols
        return bass.AP(self._a.tensor,
                       self._a.offset + r0 * self.ps + self.col0 + c0,
                       [[self.ps, r1 - r0], [1, c1 - c0]])

    def ap3(self, off, dims):
        return bass.AP(self._a.tensor, self._a.offset + self.col0 + off, dims)


def build_program(debug=False, pre_tc=PRE_TC, main_tc=MAIN_TC, do_ffn=True, stage_lim=2, scan_lim=99, tap_tci=2):
    nc = bass.Bass("TRN2", target_bir_lowering=False)

    px = nc.dram_tensor("px", [PRE_TC * TC, D], F32, kind="ExternalInput").ap()
    mx = nc.dram_tensor("mx", [MAIN_TC * TC, D], F32, kind="ExternalInput").ap()
    cvec = nc.dram_tensor("cvec", [1, D], F32, kind="ExternalInput").ap()
    flag_d = nc.dram_tensor("flag", [128, 1], F32, kind="ExternalInput").ap()
    wq_d = nc.dram_tensor("wq", [D, D], F32, kind="ExternalInput").ap()
    wk_d = nc.dram_tensor("wk", [D, 256], F32, kind="ExternalInput").ap()
    wv_d = nc.dram_tensor("wv", [D, 256], F32, kind="ExternalInput").ap()
    waux_d = nc.dram_tensor("waux", [D, AUXF], F32, kind="ExternalInput").ap()
    wcv_d = nc.dram_tensor("wcv", [1536, KW], F32, kind="ExternalInput").ap()
    alog_d = nc.dram_tensor("alog", [1, HQ], F32, kind="ExternalInput").ap()
    dtb_d = nc.dram_tensor("dtb", [1, HQ], F32, kind="ExternalInput").ap()
    nw_d = nc.dram_tensor("nw", [1, HD], F32, kind="ExternalInput").ap()
    wo_d = nc.dram_tensor("wo", [D, D], F32, kind="ExternalInput").ap()
    wr_d = nc.dram_tensor("wr", [D, D], F32, kind="ExternalInput").ap()
    wk2_d = nc.dram_tensor("wk2", [D, D], F32, kind="ExternalInput").ap()
    wv2_d = nc.dram_tensor("wv2", [D, D], F32, kind="ExternalInput").ap()
    mur_d = nc.dram_tensor("mur", [1, D], F32, kind="ExternalInput").ap()
    muk_d = nc.dram_tensor("muk", [1, D], F32, kind="ExternalInput").ap()
    wmod_d = nc.dram_tensor("wmod", [D, 6 * D], F32, kind="ExternalInput").ap()
    y_d = nc.dram_tensor("y", [2048, D], F32, kind="ExternalOutput").ap()
    scr_d = nc.dram_tensor("scr", [200, 16384], F32, kind="Internal").ap()
    SCRW = 16384

    taps = {}
    if debug:
        for nm, shape in [
            ("t_h", [128, D]), ("t_qkv", [128, QKV_CH * (TC + KW)]),
            ("t_gb", [HQ, 4 * TC]), ("t_kaug", [128, NQ * 256]),
            ("t_vu", [128, NQ * 256]), ("t_vo", [128, NQ * 256]),
            ("t_on", [128, HQ * TC]), ("t_xp", [128, D]),
            ("t_dhat", [128, 512]), ("t_qfm", [128, HQ * TC]),
            ("t_x", [128, D]), ("t_h2", [128, D]), ("t_hfm", [128, FCH * TC]),
            ("t_mv", [128, 16]), ("t_wq", [128, D]), ("t_wv", [128, 512]),
            ("t_f1", [128, FCH * (TC + 1)]), ("t_f2", [128, FCH * TC]),
            ("t_f3", [128, FCH * TC]), ("t_f4", [128, FCH * TC]),
            ("t_f5", [128, 2048]), ("t_f6", [128, 3 * D]),
            ("t_f7", [128, 3 * D]),
            ("t_misc", [128, 1024]),
        ]:
            taps[nm] = nc.dram_tensor(nm, shape, F32, kind="ExternalOutput").ap()

    cms = []

    def sbuf(name, shape):
        cm = nc.sbuf_tensor(name, shape, F32)
        t = cm.__enter__()
        cms.append(cm)
        return t

    def psum(name, shape):
        cm = nc.psum_tensor(name, shape, F32)
        t = cm.__enter__()
        cms.append(cm)
        return t

    def mksem(name):
        cm = nc.semaphore(name)
        s = cm.__enter__()
        cms.append(cm)
        return s

    warena_t = sbuf("warena", [128, 24576])
    wq_sb = R(warena_t, 0, FCH * D)
    wk_sb = R(warena_t, 8192, FCH * 256)
    wv_sb = R(warena_t, 10240, FCH * 256)
    waux_sb = R(warena_t, 12288, FCH * AUXF)
    wo_sb = R(warena_t, 13056, FCH * D)
    wfT = R(warena_t, 0, 3 * FCH * D)
    setupscr = R(warena_t, 0, 2048)

    x_sb = sbuf("x_sb", [128, 2 * D])
    csb = sbuf("csb", [128, FCH])
    flag = sbuf("flagv", [128, 1])
    modv = sbuf("modv", [128, 48])
    deriv = sbuf("deriv", [128, 48])
    dscr = sbuf("dscr", [128, 64])
    wconv = sbuf("wconv", [128, 48])
    alogc = sbuf("alogc", [128, 4])
    biasm = sbuf("biasm", [128, QKV_CH])
    biasa = sbuf("biasa", [128, 2])
    biask2 = sbuf("biask2", [128, FCH])
    ga_rep = sbuf("ga_rep", [128, D])
    gb_rep = sbuf("gb_rep", [128, D])
    br_rep = sbuf("br_rep", [128, D])
    murc = sbuf("murc", [128, FCH])
    mukc = sbuf("mukc", [128, FCH])
    i128 = sbuf("i128", [128, 128])
    onescol = sbuf("onescol", [128, 128])
    ones32 = sbuf("ones32", [128, 128])
    ones64c = sbuf("ones64c", [128, 1])
    i64r4 = sbuf("i64r4", [128, 256])
    i16p = sbuf("i16p", [128, 16])
    madd = sbuf("madd", [128, 256])
    sel2T = sbuf("sel2T", [128, 2])
    selr2p = sbuf("selr2p", [128, 128])
    selr4p = sbuf("selr4p", [128, 128])
    two_rows = sbuf("two_rows", [128, TC])
    four_rows = sbuf("four_rows", [128, TC])
    rowsc = sbuf("rowsc", [128, 1024])
    rmsrow = sbuf("rmsrow", [128, 512])
    ebpad = sbuf("ebpad", [128, 16])
    epsc = sbuf("epsc", [128, 4])
    nwc = sbuf("nwc", [128, 1])
    stat6 = sbuf("stat6", [128, 12])
    mv2 = sbuf("mv2", [128, 4])
    h_fm = sbuf("h_fm", [128, FCH * TC])
    qkv_fm_t = sbuf("qkv_fm", [128, QKV_CH * (TC + KW)])
    qkv_fm = R(qkv_fm_t, 0, QKV_CH * (TC + KW))
    kact = R(qkv_fm_t, 0, FCH * TC)
    csc1 = sbuf("csc1", [128, TC])
    csc2 = sbuf("csc2", [128, TC])
    qpair_t = sbuf("qpair", [128, FCH * TC])
    qpair = R(qpair_t, 0, FCH * TC)
    xp_tm = R(qpair_t, 0, D)
    kaug = sbuf("kaug", [128, NQ * 256])
    vparts = sbuf("vparts", [128, NQ * TC])
    katm = sbuf("katm", [128, NQ * HD])
    q_fm_t = sbuf("q_fm", [128, HQ * TC])
    q_fm = R(q_fm_t, 0, HQ * TC)
    h2_fm = R(q_fm_t, 0, FCH * (TC + 1))
    g_fm = sbuf("g_fm", [HQ, TC])
    b_fm = sbuf("b_fm", [HQ, TC])
    ehat = sbuf("ehat", [128, TC])
    eaug = sbuf("eaug", [128, HQ])
    gsp_t = sbuf("gsp_t", [HQ, TC])
    ale_t = sbuf("ale_t", [HQ, TC])
    mixscr = sbuf("mixscr", [128, TC])
    lstage = sbuf("lstage", [128, NQ * 256])
    dstage = sbuf("dstage", [128, HQ * TC])
    dhat = sbuf("dhat", [128, 256])
    amat = sbuf("amat", [128, 512])
    ufm = sbuf("ufm", [128, 256])
    ztb = sbuf("ztb", [64, 256])
    zfm = sbuf("zfm", [128, 256])
    kbarq = sbuf("kbarq", [128, NQ * 256])
    h_tm = R(kbarq, 0, D)
    vu_aug = sbuf("vu_aug", [128, NQ * 256])
    vo_aug = sbuf("vo_aug", [128, NQ * 256])
    osq = sbuf("osq", [128, 256])
    on_t = sbuf("on_t", [128, HQ * TC])
    o_n = R(on_t, 0, HQ * TC)
    xr_fm = R(on_t, 0, FCH * TC)
    xk_fm = R(on_t, FCH * TC, FCH * TC)
    o_pair = R(ga_rep, 0, FCH * TC)
    r_tm = R(gb_rep, 0, D)
    wld_t = sbuf("wld", [128, 1024])
    wld = R(wld_t, 0, 1024)
    arow = R(rowsc, 0, 1024)

    P = [psum(f"P{i}", [128, 512]) for i in range(8)]

    sems = {e: mksem("sg_" + e) for e in ENGINES}
    dsems = {"sync": mksem("sd_sync"), "gpsimd": mksem("sd_gps")}

    global SIM
    SIM = {"items": {e: [] for e in ENGINES}}
    thunks = {e: [] for e in ENGINES}
    count = {e: 0 for e in ENGINES}
    dcount = {"sync": 0, "gpsimd": 0}
    last_wait = {e: {} for e in ENGINES}
    bufs = {}

    class _B:
        __slots__ = ("w", "rs")

        def __init__(self):
            self.w = None
            self.rs = []

    def BF(name):
        if name not in bufs:
            bufs[name] = _B()
        return bufs[name]

    def _collect(reads, writes):
        deps = []
        for nm in reads:
            b = BF(nm)
            if b.w is not None:
                deps.append(b.w)
        for nm in writes:
            b = BF(nm)
            if b.w is not None:
                deps.append(b.w)
            deps.extend(b.rs)
        return deps

    def _mkwaits(eng, deps):
        need = {}
        for dp in deps:
            if dp is None:
                continue
            s, cnt = dp
            if s == eng:
                continue
            if need.get(s, 0) < cnt:
                need[s] = cnt
        waits = []
        for s, cnt in sorted(need.items()):
            if last_wait[eng].get(s, -1) >= cnt:
                continue
            waits.append((s, cnt))
            last_wait[eng][s] = cnt
        return waits

    def _commit(tok, reads, writes):
        for nm in reads:
            BF(nm).rs.append(tok)
        for nm in writes:
            b = BF(nm)
            b.w = tok
            b.rs = []

    def _emit_waits(E, waits):
        for s, cnt in waits:
            if s == "dsync":
                E.wait_ge(dsems["sync"], 16 * cnt)
            elif s == "dgps":
                E.wait_ge(dsems["gpsimd"], 16 * cnt)
            else:
                E.wait_ge(sems[s], cnt)

    def group(eng, reads=(), writes=(), fn=None):
        deps = _collect(reads, writes)
        waits = _mkwaits(eng, deps)
        count[eng] += 1
        tok = (eng, count[eng])
        SIM["items"][eng].append(
            ("grp", waits, getattr(fn, "__name__", "?"), writes))

        def thunk(E, waits=waits, fn=fn):
            _emit_waits(E, waits)
            fn(E).then_inc(sems[eng], 1)

        thunks[eng].append(thunk)
        _commit(tok, reads, writes)
        return tok

    def dma(eng, out, in_, reads=(), writes=(), slow=False):
        key = {"sync": "dsync", "gpsimd": "dgps"}[eng]
        deps = _collect(reads, writes)
        waits = _mkwaits(eng, deps)
        dcount[eng] += 1
        tok = (key, dcount[eng])
        SIM["items"][eng].append(("dma", waits, "dma", writes))

        def thunk(E, waits=waits, out=out, in_=in_, eng=eng, slow=slow):
            _emit_waits(E, waits)
            if slow:
                E.dma_start(out=out, in_=in_,
                            allow_slow_non_contiguous=True).then_inc(
                    dsems[eng], 16)
            else:
                E.dma_start(out=out, in_=in_).then_inc(dsems[eng], 16)

        thunks[eng].append(thunk)
        _commit(tok, reads, writes)
        return tok

    def ap3(tens, offset, dims):
        if isinstance(tens, R):
            return tens.ap3(offset, dims)
        a = tens[:, :]
        return bass.AP(a.tensor, a.offset + offset, dims)

    def ps_of(tens):
        if isinstance(tens, R):
            return tens.ps
        return tens[:, :].ap[0][0]

    def _iotas(E):
        E.iota(setupscr[:, 0:128], [[1, 128]], channel_multiplier=0,
               allow_small_or_imprecise_dtypes=True)
        E.iota(setupscr[:, 128:256], [[0, 128]], channel_multiplier=1,
               allow_small_or_imprecise_dtypes=True)
        E.iota(setupscr[:, 256:512], [[0, 4], [1, 64]], channel_multiplier=0,
               allow_small_or_imprecise_dtypes=True)
        E.iota(setupscr[:, 512:768], [[0, 256]], channel_multiplier=1,
               allow_small_or_imprecise_dtypes=True)
        E.iota(setupscr[:, 768:784], [[1, 16]], channel_multiplier=0,
               allow_small_or_imprecise_dtypes=True)
        E.iota(dstage[:, :], [[1, 16], [0, 128]], channel_multiplier=0,
               allow_small_or_imprecise_dtypes=True)
        E.iota(selr2p[:, :], [[1, 2], [0, 64]], channel_multiplier=0,
               allow_small_or_imprecise_dtypes=True)
        return E.iota(selr4p[:, :], [[1, 2], [0, 64]], channel_multiplier=0,
                      allow_small_or_imprecise_dtypes=True)

    group("gpsimd", writes=("setupscr", "dstage_c", "selr2p", "selr4p"),
          fn=_iotas)

    def _consts1(E):
        E.tensor_tensor(out=i128[:, :], in0=setupscr[:, 0:128],
                        in1=setupscr[:, 128:256], op=ALU.is_equal)
        E.memset(i64r4[:, :], 0.0)
        E.tensor_tensor(out=i64r4[0:64, :], in0=setupscr[0:64, 256:512],
                        in1=setupscr[0:64, 512:768], op=ALU.is_equal)
        E.memset(i16p[:, :], 0.0)
        E.tensor_tensor(out=i16p[0:16, :], in0=setupscr[0:16, 768:784],
                        in1=setupscr[0:16, 128:144], op=ALU.is_equal)
        E.memset(onescol[:, :], 0.0)
        E.memset(onescol[0:1, :], 1.0)
        E.tensor_scalar(out=ones32[:, :], in0=setupscr[:, 128:256],
                        scalar1=32.0, scalar2=None, op0=ALU.is_equal)
        E.memset(ones64c[:, 0:1], 0.0)
        E.memset(ones64c[0:64, 0:1], 1.0)
        E.memset(madd[:, :], 0.0)
        E.tensor_scalar(out=dscr[64:128, 0:64], in0=setupscr[64:128, 256:320],
                        scalar1=64.0, scalar2=None, op0=ALU.add)
        E.drain()
        E.tensor_tensor(out=madd[64:128, :], in0=setupscr[64:128, 512:768],
                        in1=ap3(dscr, 64 * ps_of(dscr),
                                [[ps_of(dscr), 64], [0, 4], [1, 64]]),
                        op=ALU.is_gt)
        E.tensor_scalar(out=madd[64:128, :], in0=madd[64:128, :],
                        scalar1=-1e9, scalar2=None, op0=ALU.mult)
        E.tensor_scalar(out=sel2T[:, 1:2], in0=setupscr[:, 128:129],
                        scalar1=63.5, scalar2=None, op0=ALU.is_gt)
        E.drain()
        E.tensor_scalar(out=sel2T[:, 0:1], in0=sel2T[:, 1:2], scalar1=-1.0,
                        scalar2=1.0, op0=ALU.mult, op1=ALU.add)
        E.tensor_scalar(out=dscr[:, 48:49], in0=setupscr[:, 128:129],
                        scalar1=0.0, scalar2=None, op0=ALU.add)
        E.drain()
        E.tensor_scalar(out=dscr[:, 49:50], in0=dscr[:, 48:49], scalar1=-2.0,
                        scalar2=None, op0=ALU.add)
        E.tensor_scalar(out=dscr[:, 50:51], in0=dscr[:, 48:49], scalar1=-32.0,
                        scalar2=None, op0=ALU.add)
        E.drain()
        E.memset(epsc[:, 0:1], EPS_LN)
        E.memset(epsc[:, 1:2], 1e-12)
        E.memset(epsc[:, 2:3], EPS_GLA)
        E.memset(epsc[:, 3:4], 0.0)
        return E.engine_nop()

    group("vector", reads=("setupscr", "dstage_c"),
          writes=("i128", "i64r4", "i16p", "onescol", "ones64c", "madd",
                  "sel2T", "selr2p", "selr4p", "dscr", "epsc", "ones32"),
          fn=_consts1)

    def _dst(E):
        E.tensor_scalar(out=dstage[:, :], in0=dstage[:, :],
                        scalar1=dscr[:, 50:51], scalar2=-1.0,
                        op0=ALU.is_equal, op1=ALU.mult)
        E.tensor_scalar(out=selr2p[:, :], in0=selr2p[:, :],
                        scalar1=dscr[:, 48:49], scalar2=None,
                        op0=ALU.is_equal)
        E.tensor_scalar(out=selr4p[:, :], in0=selr4p[:, :],
                        scalar1=dscr[:, 49:50], scalar2=None,
                        op0=ALU.is_equal)
        E.memset(two_rows[:, :], 0.0)
        E.memset(four_rows[:, :], 0.0)
        E.memset(rowsc[:, :], 0.0)
        E.memset(rmsrow[:, :], 0.0)
        E.memset(ebpad[:, :], 0.0)
        E.memset(alogc[:, :], 0.0)
        E.memset(ehat[:, :], 0.0)
        E.memset(q_fm_t[:, :], 0.0)
        E.memset(kaug[:, :], 0.0)
        E.memset(vparts[:, :], 0.0)
        E.memset(ufm[:, :], 0.0)
        E.memset(zfm[:, :], 0.0)
        E.memset(lstage[:, :], 0.0)
        E.memset(lstage[0:1, :], 1.0)
        E.memset(qkv_fm_t[:, :], 0.0)
        E.memset(vu_aug[:, :], 0.0)
        E.memset(vo_aug[:, :], 0.0)
        E.memset(katm[:, :], 0.0)
        last = None
        for blk in range(2 * NQ):
            E.tensor_copy(out=kaug[0:64, blk * 128:blk * 128 + 64],
                          in_=i64r4[0:64, 0:64])
        for kv in range(NQ):
            last = E.tensor_copy(out=katm[0:64, kv * HD:(kv + 1) * HD],
                                 in_=i64r4[0:64, 0:64])
        return last

    group("vector", reads=("dscr", "i64r4", "dstage_c", "selr2p", "selr4p"),
          writes=("dstage_c", "two_rows", "four_rows", "rowscw", "rmsrowa",
                  "rmsrowb", "rmsrowc", "ebpad", "alogL", "dtbL", "expB",
                  "negexpA", "ehat", "qfmv", "ufm", "zfm", "lstage",
                  "lstageb", "dstage0", "katmI", "kaugw0", "kaugw1",
                  "vpartsw0", "vpartsw1") +
          tuple(f"qkv{m}" for m in range(QKV_CH)) +
          tuple(f"vuk{k}" for k in range(NQ)) +
          tuple(f"vok{k}" for k in range(NQ)) +
          tuple(f"katm{k}" for k in range(NQ)) +
          tuple(f"qf{h}" for h in range(HQ)), fn=_dst)

    dma("sync", csb[:, :], ap3(cvec, 0, [[1, 128], [128, FCH]]), writes=("csb",), slow=True)
    dma("sync", flag[:, :], flag_d[:, :], writes=("flag",))
    dma("sync", alogc[0:16, 0:1], ap3(alog_d, 0, [[1, HQ], [HQ, 1]]),
        reads=("alogL",), writes=("alogL",), slow=True)
    dma("sync", alogc[0:16, 1:2], ap3(dtb_d, 0, [[1, HQ], [HQ, 1]]),
        reads=("dtbL",), writes=("dtbL",), slow=True)
    dma("sync", murc[:, :], ap3(mur_d, 0, [[1, 128], [128, FCH]]),
        writes=("murc",), slow=True)
    dma("sync", mukc[:, :], ap3(muk_d, 0, [[1, 128], [128, FCH]]),
        writes=("mukc",), slow=True)
    dma("sync", nwc[0:64, 0:1], ap3(nw_d, 0, [[1, HD], [HD, 1]]),
        writes=("nw0",), slow=True)
    dma("sync", nwc[64:128, 0:1], ap3(nw_d, 0, [[1, HD], [HD, 1]]),
        writes=("nw1",), slow=True)
    for ch in range(QKV_CH):
        dma("sync", wconv[:, ch * 4:(ch + 1) * 4],
            wcv_d[ch * 128:(ch + 1) * 128, :], writes=("wconv",))

    group("scalar", reads=("csb",), writes=("csb",),
          fn=lambda E: E.activation(out=csb[:, :], in_=csb[:, :],
                                    func=AF.Silu))
    group("scalar", reads=("alogL",), writes=("alogE",),
          fn=lambda E: E.activation(out=alogc[:, 2:3], in_=alogc[:, 0:1],
                                    func=AF.Exp))
    group("vector", reads=("alogE",), writes=("negexpA",),
          fn=lambda E: E.tensor_scalar(out=alogc[0:16, 0:1],
                                       in0=alogc[0:16, 2:3], scalar1=-1.0,
                                       scalar2=None, op0=ALU.mult))

    for m in range(12):
        for k in range(FCH):
            slot = (m * FCH + k) % 2
            dma("sync", wld[:, slot * 512:(slot + 1) * 512],
                wmod_d[k * 128:(k + 1) * 128, m * 512:(m + 1) * 512],
                reads=(f"wld{slot}",), writes=(f"wld{slot}",))

            def _mm(E, k=k, m=m, slot=slot):
                last = None
                for j in range(4):
                    cc = m * 4 + j
                    last = E.matmul(P[6][:, k * 48 + cc:k * 48 + cc + 1],
                                    wld[:, slot * 512 + j * 128:
                                        slot * 512 + (j + 1) * 128],
                                    csb[:, k:k + 1],
                                    start=True, stop=True)
                return last

            group("tensor", reads=(f"wld{slot}", "csb"), writes=("bank6",),
                  fn=_mm)

    def _modsum(E):
        E.tensor_copy(out=modv[:, :], in_=P[6][:, 0:48])
        for k in range(1, FCH):
            E.drain()
            E.tensor_tensor(out=modv[:, :], in0=modv[:, :],
                            in1=P[6][:, k * 48:(k + 1) * 48], op=ALU.add)
        return E.engine_nop()

    group("vector", reads=("bank6", "modv"), writes=("modv",), fn=_modsum)

    def _deriv(E):
        E.tensor_scalar(out=deriv[:, 0:8], in0=modv[:, 8:16], scalar1=1.0,
                        scalar2=None, op0=ALU.add)
        E.tensor_scalar(out=deriv[:, 16:24], in0=modv[:, 32:40], scalar1=1.0,
                        scalar2=None, op0=ALU.add)
        for (src, dst, s0) in ((16, 8, 0), (40, 24, 16)):
            E.tensor_scalar(out=dscr[:, s0:s0 + 8], in0=modv[:, src:src + 8],
                            scalar1=0.0, scalar2=2.0, op0=ALU.is_ge,
                            op1=ALU.mult)
            E.drain()
            E.tensor_scalar(out=dscr[:, s0:s0 + 8], in0=dscr[:, s0:s0 + 8],
                            scalar1=-1.0, scalar2=None, op0=ALU.add)
            E.tensor_scalar(out=dscr[:, s0 + 8:s0 + 16],
                            in0=modv[:, src:src + 8], scalar1=-1.0,
                            scalar2=None, op0=ALU.mult)
            E.drain()
            E.tensor_tensor(out=dscr[:, s0 + 8:s0 + 16],
                            in0=dscr[:, s0 + 8:s0 + 16],
                            in1=modv[:, src:src + 8], op=ALU.max)
            E.drain()
            E.tensor_scalar(out=dscr[:, s0 + 8:s0 + 16],
                            in0=dscr[:, s0 + 8:s0 + 16], scalar1=0.25,
                            scalar2=None, op0=ALU.max)
            E.drain()
            E.tensor_tensor(out=deriv[:, dst:dst + 8], in0=dscr[:, s0:s0 + 8],
                            in1=dscr[:, s0 + 8:s0 + 16], op=ALU.mult)
        E.drain()
        E.reciprocal(out=dscr[:, 32:40], in_=deriv[:, 0:8])
        E.drain()
        E.tensor_tensor(out=deriv[:, 32:40], in0=modv[:, 0:8],
                        in1=dscr[:, 32:40], op=ALU.mult)
        E.reciprocal(out=dscr[:, 40:48], in_=deriv[:, 16:24])
        E.drain()
        return E.tensor_tensor(out=deriv[:, 40:48], in0=modv[:, 24:32],
                               in1=dscr[:, 40:48], op=ALU.mult)

    group("vector", reads=("modv", "dscr"), writes=("deriv", "dscr"),
          fn=_deriv)

    def _nsbp(E):
        E.tensor_scalar(out=dscr[:, 51:52], in0=flag[:, 0:1], scalar1=-1.0,
                        scalar2=None, op0=ALU.add)
        E.drain()
        return E.tensor_scalar(out=dscr[:, 52:60], in0=deriv[:, 40:48],
                               scalar1=dscr[:, 51:52], scalar2=None,
                               op0=ALU.mult)

    group("vector", reads=("deriv", "flag", "dscr"), writes=("dscr",),
          fn=_nsbp)

    def _gtr(E):
        E.matmul(P[7][0:8, 0:128], deriv[:, 8:16], i128[:, :], start=True,
                 stop=True)
        return E.matmul(P[7][0:8, 128:256], deriv[:, 24:32], i128[:, :],
                        start=True, stop=True)

    group("tensor", reads=("deriv", "i128"), writes=("bank7",), fn=_gtr)
    group("scalar", reads=("bank7", "rowscw"), writes=("rowscw",),
          fn=lambda E: E.copy(out=rowsc[0:8, 0:256], in_=P[7][0:8, 0:256]))
    dma("gpsimd", scr_d[0:1, 0:1024],
        ap3(rowsc, 0, [[ps_of(rowsc), 8], [1, 128]]), reads=("rowscw",),
        writes=("scr0",))
    dma("gpsimd", scr_d[1:2, 0:1024],
        ap3(rowsc, 128, [[ps_of(rowsc), 8], [1, 128]]), reads=("rowscw",),
        writes=("scr1",))

    def rep_row(src_dram, dst_t, dst_names, src_dep):
        dma("gpsimd", rowsc[0:1, 0:1024], src_dram,
            reads=(src_dep, "rowscw"), writes=("rowscw",))
        for j in range(4):
            def _rep(E, j=j):
                return E.matmul(P[7][:, 0:256], onescol[:, 0:128],
                                rowsc[:, j * 256:(j + 1) * 256], start=True,
                                stop=True)

            group("tensor", reads=("rowscw", "onescol"), writes=("bank7",),
                  fn=_rep)
            group("scalar", reads=("bank7",), writes=(dst_names[j],),
                  fn=lambda E, j=j, dst_t=dst_t: E.copy(
                      out=dst_t[:, j * 256:(j + 1) * 256], in_=P[7][:, 0:256]))

    rep_row(scr_d[0:1, 0:1024], ga_rep,
            ["garep0", "garep1", "garep2", "garep3"], "scr0")
    rep_row(scr_d[1:2, 0:1024], gb_rep,
            ["gbrep0", "gbrep1", "gbrep2", "gbrep3"], "scr1")
    GAREPS = ("garep0", "garep1", "garep2", "garep3")
    GBREPS = ("gbrep0", "gbrep1", "gbrep2", "gbrep3")

    def load_fold(wd, wsb, ncols, name, extra_scale=None):
        for k in range(FCH):
            dma("sync", wsb[:, k * ncols:(k + 1) * ncols],
                wd[k * 128:(k + 1) * 128, :], reads=(),
                writes=(name + str(k), "setupscr"))

            def _fold(E, k=k):
                if extra_scale is not None:
                    return E.tensor_scalar(
                        out=wsb[:, k * ncols:(k + 1) * ncols],
                        in0=wsb[:, k * ncols:(k + 1) * ncols],
                        scalar1=deriv[:, k:k + 1], scalar2=extra_scale,
                        op0=ALU.mult, op1=ALU.mult)
                return E.tensor_scalar(
                    out=wsb[:, k * ncols:(k + 1) * ncols],
                    in0=wsb[:, k * ncols:(k + 1) * ncols],
                    scalar1=deriv[:, k:k + 1], scalar2=None, op0=ALU.mult)

            group("vector", reads=(name + str(k), "deriv"),
                  writes=(name + str(k),), fn=_fold)

    load_fold(wq_d, wq_sb, D, "wq")
    load_fold(wk_d, wk_sb, 256, "wk")
    load_fold(wv_d, wv_sb, 256, "wv")
    load_fold(waux_d, waux_sb, AUXF, "waux")
    for k in range(FCH):
        dma("sync", wo_sb[:, k * D:(k + 1) * D], wo_d[k * 128:(k + 1) * 128, :],
            reads=(), writes=("wo" + str(k), "setupscr"))

        def _foldwo(E, k=k):
            E.tensor_scalar(out=wo_sb[:, k * D:(k + 1) * D],
                            in0=wo_sb[:, k * D:(k + 1) * D],
                            scalar1=nwc[:, 0:1], scalar2=None, op0=ALU.mult)
            return E.tensor_tensor(out=wo_sb[:, k * D:(k + 1) * D],
                                   in0=wo_sb[:, k * D:(k + 1) * D],
                                   in1=ga_rep[:, :], op=ALU.mult)

        group("vector", reads=("wo" + str(k), "nw0", "nw1") + GAREPS,
              writes=("wo" + str(k),), fn=_foldwo)

    def bias_for(wsb, ncols, nchunks, col0, pref):
        for m in range(nchunks):
            def _bmm(E, m=m, wsb=wsb, ncols=ncols):
                last = None
                for k in range(FCH):
                    last = E.matmul(P[5][:, (m % 4):(m % 4) + 1],
                                    wsb[:, k * ncols + m * 128:
                                        k * ncols + (m + 1) * 128],
                                    deriv[:, 32 + k:33 + k],
                                    start=(k == 0), stop=(k == FCH - 1))
                return last

            group("tensor",
                  reads=tuple(pref + str(k) for k in range(FCH)) + ("deriv",),
                  writes=("bank5",), fn=_bmm)
            group("scalar", reads=("bank5",), writes=("biasm",),
                  fn=lambda E, m=m, col0=col0: E.copy(
                      out=biasm[:, col0 + m:col0 + m + 1],
                      in_=P[5][:, (m % 4):(m % 4) + 1]))

    bias_for(wq_sb, D, 8, 0, "wq")
    bias_for(wk_sb, 256, 2, 8, "wk")
    bias_for(wv_sb, 256, 2, 10, "wv")

    def _bauxmm(E):
        last = None
        for k in range(FCH):
            last = E.matmul(P[5][0:96, 8:9],
                            waux_sb[:, k * AUXF:(k + 1) * AUXF],
                            deriv[:, 32 + k:33 + k],
                            start=(k == 0), stop=(k == FCH - 1))
        return last

    group("tensor", reads=tuple("waux" + str(k) for k in range(FCH)) +
          ("deriv",), writes=("bank5",), fn=_bauxmm)

    def _bauxfin(E):
        E.tensor_copy(out=biasa[0:96, 0:1], in_=P[5][0:96, 8:9])
        E.drain()
        E.tensor_tensor(out=biasa[0:HQ, 0:1], in0=biasa[0:HQ, 0:1],
                        in1=alogc[0:16, 1:2], op=ALU.add)
        E.drain()
        return E.tensor_scalar(out=biasa[:, 1:2], in0=biasa[:, 0:1],
                               scalar1=-1.0, scalar2=None, op0=ALU.mult)

    group("vector", reads=("bank5", "dtbL"), writes=("biasa",), fn=_bauxfin)

    def ln_to_h(xa, xbuf):
        def _stats(E, xa=xa):
            E.bn_stats(out=stat6[:, 0:6], in_=xa[:, 0:512])
            E.bn_stats(out=stat6[:, 6:12], in_=xa[:, 512:1024])
            E.drain()
            E.bn_aggr(out=mv2[:, 0:2], in_=stat6[:, 0:12])
            E.drain()
            return E.engine_nop()

        group("vector", reads=(xbuf,), writes=("mv2", "stat6"), fn=_stats)

        def _is(E):
            E.activation(out=mv2[:, 2:3], in_=mv2[:, 1:2], func=AF.Ln,
                         bias=epsc[:, 0:1])
            return E.activation(out=mv2[:, 2:3], in_=mv2[:, 2:3], func=AF.Exp,
                                scale=-0.5)

        group("scalar", reads=("mv2", "epsc"), writes=("mv2",), fn=_is)

        def _apply(E, xa=xa):
            return E.tensor_scalar(out=h_tm[:, :], in0=xa[:, :],
                                   scalar1=mv2[:, 0:1], scalar2=mv2[:, 2:3],
                                   op0=ALU.subtract, op1=ALU.mult)

        group("vector", reads=("mv2", xbuf), writes=("h_tm",), fn=_apply)

    def h_transpose(dstbufa, dstbufb, cols_fn, extra_a=()):
        def _tr1(E):
            last = None
            for f in range(4):
                last = E.transpose(P[2][:, f * 128:(f + 1) * 128],
                                   h_tm[:, f * 128:(f + 1) * 128], i128[:, :])
            return last

        group("tensor", reads=("h_tm", "i128"), writes=("bank2",), fn=_tr1)

        def _ev1(E):
            last = None
            for f in range(4):
                last = E.copy(out=cols_fn(f),
                              in_=P[2][:, f * 128:(f + 1) * 128])
            return last

        group("scalar", reads=("bank2",) + extra_a, writes=(dstbufa,), fn=_ev1)

        def _tr2(E):
            last = None
            for f in range(4):
                last = E.transpose(P[2][:, f * 128:(f + 1) * 128],
                                   h_tm[:, (4 + f) * 128:(5 + f) * 128],
                                   i128[:, :])
            return last

        group("tensor", reads=("h_tm", "i128"), writes=("bank2",), fn=_tr2)

        def _ev2(E):
            last = None
            for f in range(4):
                last = E.copy(out=cols_fn(4 + f),
                              in_=P[2][:, f * 128:(f + 1) * 128])
            return last

        group("scalar", reads=("bank2",) + extra_a, writes=(dstbufb,), fn=_ev2)

    def conv_halo(chunks):
        def _halo(E):
            last = None
            for m in chunks:
                base = m * (TC + KW)
                last = E.tensor_copy(out=qkv_fm[:, base:base + KW],
                                     in_=qkv_fm[:, base + TC:base + TC + KW])
            return last

        group("gpsimd", reads=tuple(f"qkv{m}" for m in chunks),
              writes=tuple(f"qkv{m}" for m in chunks), fn=_halo)

    def project(chunks):
        for i, m in enumerate(chunks):
            if m < 8:
                wsb, ncols, pref = wq_sb, D, "wq"
                fo = m * 128
            elif m < 10:
                wsb, ncols, pref = wk_sb, 256, "wk"
                fo = (m - 8) * 128
            else:
                wsb, ncols, pref = wv_sb, 256, "wv"
                fo = (m - 10) * 128
            bank = i % 2

            def _pmm(E, wsb=wsb, ncols=ncols, fo=fo, bank=bank):
                last = None
                for k in range(FCH):
                    last = E.matmul(P[bank][:, 0:TC],
                                    wsb[:, k * ncols + fo: k * ncols + fo + 128],
                                    h_fm[:, k * TC:(k + 1) * TC],
                                    start=(k == 0), stop=(k == FCH - 1))
                return last

            group("tensor",
                  reads=tuple(pref + str(k) for k in range(FCH)) +
                  ("h_fm", "h_fma"),
                  writes=(f"bank{bank}",), fn=_pmm)

            def _pev(E, m=m, bank=bank):
                return E.activation(
                    out=qkv_fm[:, m * (TC + KW) + KW:(m + 1) * (TC + KW)],
                    in_=P[bank][:, 0:TC], func=AF.Identity,
                    bias=biasm[:, m:m + 1], scale=1.0)

            group("scalar", reads=(f"bank{bank}", "biasm"),
                  writes=(f"qkv{m}",), fn=_pev)

        def _paux(E):
            last = None
            for k in range(FCH):
                last = E.matmul(P[4][0:AUXF, 0:TC],
                                waux_sb[:, k * AUXF:(k + 1) * AUXF],
                                h_fm[:, k * TC:(k + 1) * TC],
                                start=(k == 0), stop=(k == FCH - 1))
            return last

        group("tensor",
              reads=tuple("waux" + str(k) for k in range(FCH)) +
              ("h_fm", "h_fma"),
              writes=("bank4",), fn=_paux)

    def conv_one(m):
        base = m * (TC + KW)

        def _conv(E, m=m, base=base):
            E.tensor_scalar(out=csc1[:, :],
                            in0=qkv_fm[:, base + 1: base + 1 + TC],
                            scalar1=wconv[:, m * 4:m * 4 + 1], scalar2=None,
                            op0=ALU.mult)
            E.scalar_tensor_tensor(out=csc2[:, :],
                                   in0=qkv_fm[:, base + 2: base + 2 + TC],
                                   scalar=wconv[:, m * 4 + 1:m * 4 + 2],
                                   in1=csc1[:, :], op0=ALU.mult, op1=ALU.add)
            E.scalar_tensor_tensor(out=csc1[:, :],
                                   in0=qkv_fm[:, base + 3: base + 3 + TC],
                                   scalar=wconv[:, m * 4 + 2:m * 4 + 3],
                                   in1=csc2[:, :], op0=ALU.mult, op1=ALU.add)
            return E.scalar_tensor_tensor(
                out=csc2[:, :], in0=qkv_fm[:, base + 4: base + 4 + TC],
                scalar=wconv[:, m * 4 + 3:m * 4 + 4], in1=csc1[:, :],
                op0=ALU.mult, op1=ALU.add)

        group("vector", reads=(f"qkv{m}", "wconv"), writes=("csc1", "csc2"),
              fn=_conv)

    def knorm(kc, zero_kv):
        group("scalar", reads=("csc2",), writes=("csc1",),
              fn=lambda E: E.activation(out=csc1[:, :], in_=csc2[:, :],
                                        func=AF.Silu))

        def _ksq(E):
            return E.tensor_tensor(out=csc2[:, :], in0=csc1[:, :],
                                   in1=csc1[:, :], op=ALU.mult)

        group("vector", reads=("csc1",), writes=("csc2",), fn=_ksq)

        def _ssq(E):
            return E.matmul(P[3][0:2, 0:TC], sel2T[:, 0:2], csc2[:, :],
                            start=True, stop=True)

        group("tensor", reads=("csc2", "sel2T"), writes=("bank3",), fn=_ssq)

        def _nrm(E):
            E.activation(out=two_rows[0:2, :], in_=P[3][0:2, 0:TC],
                         func=AF.Ln, bias=epsc[0:2, 1:2])
            return E.activation(out=two_rows[0:2, :], in_=two_rows[0:2, :],
                                func=AF.Exp, scale=0.5)

        group("scalar", reads=("bank3", "two_rows", "epsc"),
              writes=("two_rows",), fn=_nrm)

        def _rcp(E):
            E.tensor_scalar(out=two_rows[0:2, :], in0=two_rows[0:2, :],
                            scalar1=EPS_GLA, scalar2=None, op0=ALU.add)
            return E.reciprocal(out=two_rows[0:2, :], in_=two_rows[0:2, :])

        group("vector", reads=("two_rows",), writes=("two_rows",), fn=_rcp)

        def _rep(E):
            return E.matmul(P[3][:, 192:192 + TC], selr2p[:, :],
                            two_rows[:, :], start=True, stop=True)

        group("tensor", reads=("two_rows", "selr2p"), writes=("bank3",),
              fn=_rep)

        def _kn(E, kc=kc, zero_kv=zero_kv):
            kv0, kv1 = 2 * kc, 2 * kc + 1
            kp = ps_of(kaug)
            cp = ps_of(csc1)
            _p3 = P[3][0:1, 0:1]
            pp = P[3][:, :].ap[0][0]
            E.tensor_tensor(
                out=ap3(kaug, kv0 * 256 + 64, [[kp, 64], [128, 2], [1, 64]]),
                in0=ap3(csc1, 0, [[cp, 64], [64, 2], [1, 64]]),
                in1=bass.AP(_p3.tensor, _p3.offset + 192,
                            [[pp, 64], [64, 2], [1, 64]]), op=ALU.mult)
            last = E.tensor_tensor(
                out=ap3(kaug, kv1 * 256 + 64, [[kp, 64], [128, 2], [1, 64]]),
                in0=ap3(csc1, 64 * cp, [[cp, 64], [64, 2], [1, 64]]),
                in1=bass.AP(_p3.tensor, _p3.offset + 64 * pp + 192,
                            [[pp, 64], [64, 2], [1, 64]]), op=ALU.mult)
            if zero_kv:
                last = E.tensor_scalar(
                    out=ap3(kaug, kv0 * 256 + 64,
                            [[kp, 64], [128, 4], [1, 64]]),
                    in0=ap3(kaug, kv0 * 256 + 64,
                            [[kp, 64], [128, 4], [1, 64]]),
                    scalar1=flag[0:64, 0:1], scalar2=None, op0=ALU.mult)
            return last

        group("vector", reads=("csc1", "bank3", "flag"),
              writes=(f"kaugw{kc}",), fn=_kn)

    def vbeta(vc, zero_kv):
        group("scalar", reads=("csc2",), writes=("csc1",),
              fn=lambda E: E.activation(out=csc1[:, :], in_=csc2[:, :],
                                        func=AF.Silu))
        if vc == 0:
            def _beta(E):
                return E.activation(out=four_rows[0:4, :],
                                    in_=P[4][64:68, 0:TC], func=AF.Exp,
                                    scale=-1.0, bias=biasa[64:68, 1:2])

            group("scalar", reads=("bank4", "biasa", "four_rows"),
                  writes=("four_rows",), fn=_beta)

            def _b2(E):
                E.tensor_scalar(out=four_rows[0:4, :], in0=four_rows[0:4, :],
                                scalar1=1.0, scalar2=None, op0=ALU.add)
                return E.reciprocal(out=four_rows[0:4, :],
                                    in_=four_rows[0:4, :])

            group("vector", reads=("four_rows",), writes=("four_rows",),
                  fn=_b2)

        sel = selr2p if vc == 0 else selr4p

        def _brep(E, sel=sel):
            return E.matmul(P[3][:, 320:320 + TC], sel[:, :], four_rows[:, :],
                            start=True, stop=True)

        group("tensor", reads=("four_rows", "selr2p", "selr4p"),
              writes=("bank3",), fn=_brep)

        def _vb(E, vc=vc, zero_kv=zero_kv):
            kv0, kv1 = 2 * vc, 2 * vc + 1
            E.tensor_tensor(out=vparts[0:64, kv0 * TC:(kv0 + 1) * TC],
                            in0=csc1[0:64, :], in1=P[3][0:64, 320:320 + TC],
                            op=ALU.mult)
            last = E.tensor_tensor(out=vparts[0:64, kv1 * TC:(kv1 + 1) * TC],
                                   in0=csc1[64:128, :],
                                   in1=P[3][64:128, 320:320 + TC],
                                   op=ALU.mult)
            if zero_kv:
                last = E.tensor_scalar(
                    out=vparts[0:64, kv0 * TC:(kv1 + 1) * TC],
                    in0=vparts[0:64, kv0 * TC:(kv1 + 1) * TC],
                    scalar1=flag[0:64, 0:1], scalar2=None, op0=ALU.mult)
            return last

        group("vector", reads=("csc1", "bank3", "flag"),
              writes=(f"vpartsw{vc}",), fn=_vb)

    def g_pipe(is_prefix):
        def _g1(E):
            E.activation(out=gsp_t[:, :], in_=P[4][0:HQ, 0:TC], func=AF.Exp,
                         bias=biasa[0:HQ, 0:1])
            return E.activation(out=gsp_t[:, :], in_=gsp_t[:, :], func=AF.Ln,
                                bias=1.0)

        group("scalar", reads=("bank4", "biasa", "gsp"), writes=("gsp",),
              fn=_g1)

        def _g2(E):
            E.tensor_scalar(out=g_fm[:, :], in0=gsp_t[:, :],
                            scalar1=alogc[0:16, 0:1], scalar2=None,
                            op0=ALU.mult)
            E.tensor_tensor_scan(out=b_fm[:, 0:C], data0=g_fm[:, 0:C],
                                 data1=g_fm[:, 0:C], initial=0.0, op0=ALU.add,
                                 op1=ALU.bypass)
            return E.tensor_tensor_scan(out=b_fm[:, C:2 * C],
                                        data0=g_fm[:, C:2 * C],
                                        data1=g_fm[:, C:2 * C], initial=0.0,
                                        op0=ALU.add, op1=ALU.bypass)

        group("vector", reads=("gsp", "negexpA"), writes=("b_fm", "g_fmw"),
              fn=_g2)

        def _ehat(E):
            E.activation(out=ehat[0:16, 0:C], in_=b_fm[:, 0:C], func=AF.Exp,
                         scale=-1.0, bias=b_fm[:, C - 1:C])
            return E.activation(out=ehat[0:16, C:2 * C], in_=b_fm[:, C:2 * C],
                                func=AF.Exp, scale=-1.0,
                                bias=b_fm[:, 2 * C - 1:2 * C])

        group("scalar", reads=("b_fm", "ehat"), writes=("ehat",), fn=_ehat)

        if not is_prefix:
            def _al(E):
                return E.activation(out=ale_t[:, :], in_=P[4][32:48, 0:TC],
                                    func=AF.Exp, scale=-1.0,
                                    bias=biasa[32:48, 1:2])

            group("scalar", reads=("bank4", "biasa", "ale"), writes=("ale",),
                  fn=_al)

            def _al2(E):
                E.tensor_scalar(out=ale_t[:, :], in0=ale_t[:, :], scalar1=1.0,
                                scalar2=None, op0=ALU.add)
                return E.reciprocal(out=ale_t[:, :], in_=ale_t[:, :])

            group("vector", reads=("ale",), writes=("ale",), fn=_al2)
            dma("gpsimd", scr_d[2:3, 0:HQ * TC],
                ap3(ale_t, 0, [[ps_of(ale_t), HQ], [1, TC]]), reads=("ale",),
                writes=("scr2",))
            dma("gpsimd", rowsc[0:1, 0:1024], scr_d[2:3, 0:FCH * TC],
                reads=("scr2", "rowscw"), writes=("rowscw",))
            dma("gpsimd", rowsc[32:33, 0:1024], scr_d[2:3, FCH * TC:HQ * TC],
                reads=("scr2", "rowscw"), writes=("rowscw",))
            dma("gpsimd", scr_d[3:4, 0:HQ * TC],
                ap3(b_fm, 0, [[ps_of(b_fm), HQ], [1, TC]]), reads=("b_fm",),
                writes=("scr3",))
            dma("gpsimd", dstage[0:1, :], scr_d[3:4, 0:HQ * TC],
                reads=("scr3", "dstage_c"), writes=("dstage0",))

            def _lst(E):
                E.drain()
                last = None
                for kv in range(NQ):
                    last = E.tensor_copy(
                        out=ap3(lstage, 32 * ps_of(lstage) + kv * 256 + C,
                                [[ps_of(lstage), 16], [128, 2], [1, C]]),
                        in_=ap3(b_fm, 0,
                                [[ps_of(b_fm), 16], [C, 2], [1, C]]))
                return last

            group("vector", reads=("b_fm", "lstage"), writes=("lstageb",),
                  fn=_lst)

    def scan_chunk(cc, is_prefix, scan_lim=99):
        kps = ps_of(kaug)

        def _eb(E, cc=cc):
            return E.activation(out=alogc[0:16, 3:4],
                                in_=b_fm[:, (cc + 1) * C - 1:(cc + 1) * C],
                                func=AF.Exp)

        group("scalar", reads=("b_fm", "expB"), writes=("expB",), fn=_eb)

        def _etm(E, cc=cc):
            E.matmul(P[3][0:C, 0:16], ehat[:, cc * C:(cc + 1) * C],
                     i16p[:, :], start=True, stop=True)
            return E.matmul(P[3][0:1, 16:32], alogc[:, 3:4], i16p[:, :],
                            start=True, stop=True)

        group("tensor", reads=("ehat", "expB", "i16p"), writes=("bank3",),
              fn=_etm)

        def _ebrow(E):
            return E.tensor_copy(out=ebpad[0:1, 0:16], in_=P[3][0:1, 16:32])

        group("vector", reads=("bank3", "ebpad"), writes=("ebpad",),
              fn=_ebrow)

        def _ebbc(E):
            return E.matmul(P[3][0:64, 32:48], onescol[:, 0:64], ebpad[:, :],
                            start=True, stop=True)

        group("tensor", reads=("ebpad", "onescol"), writes=("bank3",),
              fn=_ebbc)

        def _eaug(E):
            E.copy(out=eaug[0:64, :], in_=P[3][0:64, 32:48])
            return E.copy(out=eaug[64:128, :], in_=P[3][0:C, 0:16])

        group("scalar", reads=("bank3",), writes=("eaug",), fn=_eaug)

        for kv in range(NQ):
            def _trep(E, kv=kv, cc=cc):
                E.matmul(P[2][0:64, 0:256],
                         kaug[:, (kv * 2 + cc) * 128 + 64:
                              (kv * 2 + cc) * 128 + 128],
                         i64r4[:, :], start=True, stop=True)
                return E.matmul(P[2][64:128, 0:256],
                                vparts[:, kv * TC + cc * C:
                                       kv * TC + (cc + 1) * C],
                                i64r4[:, :], start=True, stop=True)

            group("tensor", reads=(f"kaugw{kv // 2}", f"vpartsw{kv // 2}",
                                   "i64r4"),
                  writes=("bank2",), fn=_trep)

            def _trev(E, kv=kv):
                E.copy(out=vu_aug[64:128, kv * 256:(kv + 1) * 256],
                       in_=P[2][0:64, 0:256])
                E.copy(out=vo_aug[64:128, kv * 256:(kv + 1) * 256],
                       in_=P[2][64:128, 0:256])
                return E.copy(out=katm[64:128, kv * HD:(kv + 1) * HD],
                              in_=P[2][0:64, 0:64])

            group("scalar", reads=("bank2",),
                  writes=(f"vuk{kv}", f"vok{kv}", f"katm{kv}"), fn=_trev)

        for kv in range(NQ):
            eng = "vector" if kv < 2 else "gpsimd"

            def _kbar(E, kv=kv):
                last = None
                for j in range(4):
                    h = 4 * kv + j
                    last = E.tensor_scalar(
                        out=kbarq[:, kv * 256 + j * HD: kv * 256 + (j + 1) * HD],
                        in0=katm[:, kv * HD:(kv + 1) * HD],
                        scalar1=eaug[:, h:h + 1], scalar2=None, op0=ALU.mult)
                return last

            group(eng, reads=(f"katm{kv}", "katmI", "eaug"),
                  writes=(f"kbar{kv}", "h_tm"), fn=_kbar)

        if not is_prefix:
            for kv in range(NQ):
                def _s1(E, kv=kv, cc=cc):
                    lhs = kaug[:, (kv * 2 + cc) * 128:(kv * 2 + cc + 1) * 128]
                    rhs = ap3(q_fm, 4 * kv * TC + cc * C,
                              [[ps_of(q_fm), 128], [TC, 4], [1, C]])
                    return E.matmul(P[0][:, 0:256], lhs, rhs, start=True,
                                    stop=True)

                group("tensor", reads=(f"kaugw{kv // 2}", "qfmv"),
                      writes=("bank0",), fn=_s1)
                if scan_lim < 2:
                    continue

                def _dmm(E, kv=kv, cc=cc):
                    lhs = lstage[:, (kv * 2 + cc) * 128:
                                 (kv * 2 + cc + 1) * 128]
                    rhs = ap3(dstage, 4 * kv * TC + cc * C,
                              [[ps_of(dstage), 128], [TC, 4], [1, C]])
                    return E.matmul(P[1][:, 0:256], lhs, rhs, start=True,
                                    stop=True)

                group("tensor", reads=("lstageb", "dstage0", "dstage_c",
                                       "lstage"),
                      writes=("bank1",), fn=_dmm)

                def _dh1(E):
                    return E.tensor_tensor(out=dhat[:, 0:256],
                                           in0=P[1][:, 0:256],
                                           in1=madd[:, :], op=ALU.add)

                group("vector", reads=("bank1", "madd"), writes=("dhat",),
                      fn=_dh1)

                def _dh2(E):
                    return E.activation(out=dhat[:, 0:256],
                                        in_=dhat[:, 0:256], func=AF.Exp)

                group("scalar", reads=("dhat",), writes=("dhat",), fn=_dh2)
                if scan_lim < 3:
                    continue

                def _a1(E):
                    return E.tensor_tensor(out=amat[:, 0:256],
                                           in0=P[0][:, 0:256],
                                           in1=dhat[:, 0:256], op=ALU.mult)

                group("vector", reads=("bank0", "dhat"), writes=("amatA",),
                      fn=_a1)
                if scan_lim < 21:
                    continue

                def _umm(E, kv=kv):
                    last = None
                    for j in range(4):
                        last = E.matmul(
                            P[6][0:64, j * C:(j + 1) * C],
                            vu_aug[:, kv * 256 + j * HD: kv * 256 + (j + 1) * HD],
                            amat[:, j * C:(j + 1) * C], start=True, stop=True)
                    return last

                group("tensor", reads=("amatA", f"vuk{kv}"), writes=("bank6",),
                      fn=_umm)
                if scan_lim < 22:
                    continue

                group("vector", reads=("bank6", "ufm"), writes=("ufm",),
                      fn=lambda E: E.tensor_copy(out=ufm[0:64, :],
                                                 in_=P[6][0:64, 0:256]))
                if scan_lim < 23:
                    continue

                def _zt(E, kv=kv, cc=cc):
                    qs = ap3(q_fm, 4 * kv * TC + cc * C,
                             [[ps_of(q_fm), 64], [TC, 4], [1, C]])
                    return E.scalar_tensor_tensor(out=ztb[:, :],
                                                  in0=P[6][0:64, 0:256],
                                                  scalar=-DAMP, in1=qs,
                                                  op0=ALU.mult, op1=ALU.add)

                group("vector", reads=("bank6", "qfmv"), writes=("ztb",),
                      fn=_zt)
                if scan_lim < 4:
                    continue

                def _s2(E, kv=kv, cc=cc):
                    lhs = kaug[:, (kv * 2 + cc) * 128:(kv * 2 + cc + 1) * 128]
                    return E.matmul(P[0][:, 256:512], lhs, ufm[:, :],
                                    start=True, stop=True)

                group("tensor", reads=(f"kaugw{kv // 2}", "ufm"),
                      writes=("bank0",), fn=_s2)

                def _a2(E):
                    return E.tensor_tensor(out=amat[:, 256:512],
                                           in0=P[0][:, 256:512],
                                           in1=dhat[:, 0:256], op=ALU.mult)

                group("vector", reads=("bank0", "dhat"), writes=("amatB",),
                      fn=_a2)

                def _wmm(E, kv=kv):
                    last = None
                    for j in range(4):
                        last = E.matmul(
                            P[7][0:64, j * C:(j + 1) * C],
                            vu_aug[:, kv * 256 + j * HD: kv * 256 + (j + 1) * HD],
                            amat[:, 256 + j * C:256 + (j + 1) * C],
                            start=True, stop=True)
                    return last

                group("tensor", reads=("amatB", f"vuk{kv}"), writes=("bank7",),
                      fn=_wmm)

                def _z(E):
                    return E.scalar_tensor_tensor(out=zfm[0:64, :],
                                                  in0=P[7][0:64, 0:256],
                                                  scalar=DAMP * DAMP,
                                                  in1=ztb[:, :], op0=ALU.mult,
                                                  op1=ALU.add)

                group("vector", reads=("bank7", "ztb", "zfm"), writes=("zfm",),
                      fn=_z)
                if scan_lim < 5:
                    continue

                def _s3(E, kv=kv, cc=cc):
                    lhs = kaug[:, (kv * 2 + cc) * 128:(kv * 2 + cc + 1) * 128]
                    return E.matmul(P[0][:, 0:256], lhs, zfm[:, :],
                                    start=True, stop=True)

                group("tensor", reads=(f"kaugw{kv // 2}", "zfm", "amatA"),
                      writes=("bank0",), fn=_s3)

                def _a3(E):
                    return E.tensor_tensor(out=amat[:, 0:256],
                                           in0=P[0][:, 0:256],
                                           in1=dhat[:, 0:256], op=ALU.mult)

                group("vector", reads=("bank0", "dhat", "bank6"),
                      writes=("amatA",), fn=_a3)

                def _omm(E, kv=kv):
                    last = None
                    for j in range(4):
                        last = E.matmul(
                            P[6][0:64, j * C:(j + 1) * C],
                            vo_aug[:, kv * 256 + j * HD: kv * 256 + (j + 1) * HD],
                            amat[:, j * C:(j + 1) * C], start=True, stop=True)
                    return last

                group("tensor", reads=("amatA", f"vok{kv}", "ufm", "ztb"),
                      writes=("bank6",), fn=_omm)
                if scan_lim < 6:
                    continue

                def _osq(E):
                    return E.activation(out=amat[0:64, 0:256],
                                        in_=P[6][0:64, 0:256], func=AF.Square)

                group("scalar", reads=("bank6",), writes=("amatA",), fn=_osq)

                def _ssum(E):
                    return E.matmul(P[3][0:1, 256:512], ones64c[:, 0:1],
                                    amat[:, 0:256], start=True, stop=True)

                group("tensor", reads=("amatA", "ones64c"), writes=("bank3",),
                      fn=_ssum)

                def _aa(E, kv=kv, cc=cc):
                    rb = (kv // 2) * 32
                    al = ap3(arow, rb * ps_of(arow) +
                             (kv % 2) * 4 * TC + cc * C,
                             [[ps_of(arow), 1], [TC, 4], [1, C]])
                    E.tensor_tensor(out=rmsrow[rb:rb + 1, 0:256], in0=al,
                                    in1=al, op=ALU.mult)
                    return E.tensor_tensor(out=rmsrow[rb:rb + 1, 0:256],
                                           in0=rmsrow[rb:rb + 1, 0:256],
                                           in1=P[3][0:1, 256:512],
                                           op=ALU.mult)

                group("vector", reads=("bank3", "rowscw", "rmsrowa"),
                      writes=("rmsrowa",), fn=_aa)

                def _rstd(E, kv=kv):
                    rb = (kv // 2) * 32
                    E.activation(out=rmsrow[rb:rb + 1, 256:512],
                                 in_=rmsrow[rb:rb + 1, 0:256], func=AF.Ln,
                                 scale=1.0 / HD, bias=epsc[rb:rb + 1, 2:3])
                    return E.activation(out=rmsrow[rb:rb + 1, 256:512],
                                        in_=rmsrow[rb:rb + 1, 256:512],
                                        func=AF.Exp, scale=-0.5)

                group("scalar", reads=("rmsrowa", "rmsrowb", "epsc"),
                      writes=("rmsrowb",), fn=_rstd)

                def _scl(E, kv=kv, cc=cc):
                    rb = (kv // 2) * 32
                    al = ap3(arow, rb * ps_of(arow) +
                             (kv % 2) * 4 * TC + cc * C,
                             [[ps_of(arow), 1], [TC, 4], [1, C]])
                    return E.tensor_tensor(out=rmsrow[rb:rb + 1, 0:256],
                                           in0=rmsrow[rb:rb + 1, 256:512],
                                           in1=al, op=ALU.mult)

                group("vector", reads=("rmsrowb", "rowscw", "rmsrowc"),
                      writes=("rmsrowc",), fn=_scl)

                def _srep(E, kv=kv):
                    sel = onescol if kv < 2 else ones32
                    return E.matmul(P[2][0:64, 256:512], sel[:, 0:64],
                                    rmsrow[:, 0:256], start=True, stop=True)

                group("tensor", reads=("rmsrowc", "onescol", "ones32"),
                      writes=("bank2",), fn=_srep)

                group("scalar", reads=("bank2",), writes=("dhat",),
                      fn=lambda E: E.copy(out=dhat[0:64, :],
                                          in_=P[2][0:64, 256:512]))

                def _on(E, kv=kv, cc=cc):
                    return E.tensor_tensor(
                        out=ap3(o_n, 4 * kv * TC + cc * C,
                                [[ps_of(o_n), 64], [TC, 4], [1, C]]),
                        in0=P[6][0:64, 0:256], in1=dhat[0:64, :],
                        op=ALU.mult)

                group("vector", reads=("bank6", "dhat"), writes=(f"on{kv}",),
                      fn=_on)

        for kv in range(NQ):
            def _st(E, kv=kv):
                last = None
                for j in range(4):
                    h8 = (4 * kv + j) % 8
                    last = E.matmul(
                        P[5][0:64, h8 * HD:(h8 + 1) * HD],
                        kbarq[:, kv * 256 + j * HD: kv * 256 + (j + 1) * HD],
                        vu_aug[:, kv * 256 + j * HD: kv * 256 + (j + 1) * HD],
                        start=True, stop=True)
                    last = E.matmul(
                        P[5][64:128, h8 * HD:(h8 + 1) * HD],
                        kbarq[:, kv * 256 + j * HD: kv * 256 + (j + 1) * HD],
                        vo_aug[:, kv * 256 + j * HD: kv * 256 + (j + 1) * HD],
                        start=True, stop=True)
                return last

            rd = (f"kbar{kv}", f"vuk{kv}", f"vok{kv}", "h_tm")
            if not is_prefix:
                rd = rd + ("bank6", "bank7")
            group("tensor", reads=rd, writes=("bank5",), fn=_st)

            def _stev(E, kv=kv):
                c0 = (kv % 2) * 256
                E.tensor_copy(out=vu_aug[0:64, kv * 256:(kv + 1) * 256],
                              in_=P[5][0:64, c0:c0 + 256])
                return E.tensor_copy(out=vo_aug[0:64, kv * 256:(kv + 1) * 256],
                                     in_=P[5][64:128, c0:c0 + 256])

            group("vector", reads=("bank5",), writes=(f"vuk{kv}", f"vok{kv}"),
                  fn=_stev)

    for tci in range(pre_tc):
        slot = tci % 2
        xa = x_sb[:, slot * D:(slot + 1) * D]
        xb = f"xslot{slot}"
        dma("sync", xa, px[tci * TC:(tci + 1) * TC, :], reads=(xb,),
            writes=(xb,))
        ln_to_h(xa, xb)
        conv_halo(range(8, QKV_CH))
        h_transpose("h_fma", "h_fm", lambda f: h_fm[:, f * TC:(f + 1) * TC])
        project(range(8, QKV_CH))
        for m in range(8, 10):
            conv_one(m)
            knorm(m - 8, True)
        for m in range(10, QKV_CH):
            conv_one(m)
            vbeta(m - 10, True)
        g_pipe(True)
        for cc in range(2):
            scan_chunk(cc, True)

    def _stflag(E):
        E.tensor_scalar(out=vu_aug[0:64, :], in0=vu_aug[0:64, :],
                        scalar1=flag[0:64, 0:1], scalar2=None, op0=ALU.mult)
        return E.tensor_scalar(out=vo_aug[0:64, :], in0=vo_aug[0:64, :],
                               scalar1=flag[0:64, 0:1], scalar2=None,
                               op0=ALU.mult)

    group("vector", reads=("flag",) + tuple(f"vuk{k}" for k in range(NQ)) +
          tuple(f"vok{k}" for k in range(NQ)),
          writes=tuple(f"vuk{k}" for k in range(NQ)) +
          tuple(f"vok{k}" for k in range(NQ)), fn=_stflag)

    for tci in range(main_tc):
        slot = tci % 2
        xa = x_sb[:, slot * D:(slot + 1) * D]
        xb = f"xslot{slot}"
        dma("sync", xa, mx[tci * TC:(tci + 1) * TC, :], reads=(xb,),
            writes=(xb,))
        if debug and tci == tap_tci:
            dma("gpsimd", taps["t_x"][:, :], xa, reads=(xb,))
        ln_to_h(xa, xb)
        if debug and tci == tap_tci:
            dma("gpsimd", taps["t_h2"][:, :], h_tm[:, :], reads=("h_tm",))
            dma("gpsimd", taps["t_mv"][:, 0:4], mv2[:, :],
                reads=("mv2",))
            dma("gpsimd", taps["t_mv"][:, 4:16], stat6[:, :],
                reads=("stat6",))
            dma("gpsimd", taps["t_misc"][:, 0:48], modv[:, :],
                reads=("modv",))
            dma("gpsimd", taps["t_misc"][:, 48:96], deriv[:, :],
                reads=("deriv",))
            dma("gpsimd", taps["t_misc"][:, 96:108], biasm[:, :],
                reads=("biasm",))
            dma("gpsimd", taps["t_misc"][:, 112:114], biasa[:, :],
                reads=("biasa",))
            dma("gpsimd", taps["t_wq"][:, :], wq_sb[:, 0:D],
                reads=("wq0",))
            dma("gpsimd", taps["t_wv"][:, 0:256], wv_sb[:, 0:256],
                reads=("wv0",))
            dma("gpsimd", taps["t_wv"][:, 256:352], waux_sb[:, 0:AUXF],
                reads=("waux0",))
            dma("gpsimd", taps["t_wv"][:, 352:448], waux_sb[:, 4 * AUXF:5 * AUXF],
                reads=("waux4",))
        conv_halo(range(QKV_CH))
        h_transpose("h_fma", "h_fm", lambda f: h_fm[:, f * TC:(f + 1) * TC])
        if debug and tci == tap_tci:
            dma("gpsimd", taps["t_hfm"][:, :], h_fm[:, :],
                reads=("h_fm", "h_fma"))
        project(range(QKV_CH))
        for m in range(8):
            conv_one(m)
            group("scalar", reads=("csc2",), writes=(f"qp{m}",),
                  fn=lambda E, m=m: E.activation(
                      out=qpair[:, m * TC:(m + 1) * TC], in_=csc2[:, :],
                      func=AF.Silu))
            group("vector", reads=(f"qp{m}",), writes=(f"qp{m}",),
                  fn=lambda E, m=m: E.tensor_scalar(
                      out=qpair[:, m * TC:(m + 1) * TC],
                      in0=qpair[:, m * TC:(m + 1) * TC],
                      scalar1=float(HD) ** -0.5, scalar2=None, op0=ALU.mult))
        for m in range(8, 10):
            conv_one(m)
            knorm(m - 8, tci == 0)
        for m in range(10, QKV_CH):
            conv_one(m)
            vbeta(m - 10, tci == 0)

        for mp in range(FCH):
            dma("gpsimd", scr_d[4 + mp:5 + mp, 0:128 * TC],
                ap3(qpair, mp * TC, [[ps_of(qpair), 128], [1, TC]]),
                reads=(f"qp{mp}",), writes=(f"scrq{mp}",))
        for mp in range(FCH):
            dma("gpsimd",
                ap3(q_fm, (2 * mp) * TC, [[ps_of(q_fm), 64], [1, TC]]),
                scr_d[4 + mp:5 + mp, 0:64 * TC],
                reads=(f"scrq{mp}", "qfmv"), writes=(f"qf{2 * mp}",))
            dma("gpsimd",
                ap3(q_fm, (2 * mp + 1) * TC, [[ps_of(q_fm), 64], [1, TC]]),
                scr_d[4 + mp:5 + mp, 64 * TC:128 * TC],
                reads=(f"scrq{mp}", "qfmv"), writes=(f"qf{2 * mp + 1}",))
        group("gpsimd", reads=tuple(f"qf{h}" for h in range(HQ)),
              writes=("qfmv",), fn=lambda E: E.engine_nop())

        g_pipe(False)
        for cc in range(2):
            scan_chunk(cc, not (stage_lim >= 1), scan_lim)
        if debug and tci == tap_tci:
            dma("gpsimd", taps["t_kaug"][:, :], kaug[:, :],
                reads=("kaugw0", "kaugw1"))
            dma("gpsimd", taps["t_vu"][:, :], vu_aug[:, :],
                reads=tuple(f"vuk{k}" for k in range(NQ)))
            dma("gpsimd", taps["t_vo"][:, :], vo_aug[:, :],
                reads=tuple(f"vok{k}" for k in range(NQ)))
            dma("gpsimd", taps["t_on"][:, :], on_t[:, :],
                reads=tuple(f"on{k}" for k in range(NQ)))
            dma("gpsimd", taps["t_qfm"][:, :], q_fm_t[:, :], reads=("qfmv",))
            dma("gpsimd", taps["t_qkv"][:, :], qkv_fm_t[:, :],
                reads=tuple(f"qkv{m}" for m in range(QKV_CH)))
            dma("gpsimd", taps["t_gb"][0:16, 0:TC], g_fm[:, :],
                reads=("g_fmw",))
            dma("gpsimd", taps["t_gb"][0:16, TC:2 * TC], b_fm[:, :],
                reads=("b_fm",))
            dma("gpsimd", taps["t_gb"][0:16, 2 * TC:3 * TC], ehat[0:16, :],
                reads=("ehat",))
            dma("gpsimd", taps["t_dhat"][:, 0:256], dhat[:, :],
                reads=("dhat",))
        if stage_lim < 2:
            continue

        for h in range(HQ):
            dma("gpsimd",
                scr_d[20 + h // 2:21 + h // 2,
                      (h % 2) * 64 * TC:((h % 2) + 1) * 64 * TC],
                ap3(o_n, h * TC, [[ps_of(o_n), 64], [1, TC]]),
                reads=(f"on{h // 4}",), writes=(f"scro{h}",))
        for pp in range(FCH):
            extra_w = GAREPS if pp == 0 else ()
            dma("gpsimd", o_pair[:, pp * TC:(pp + 1) * TC],
                ap3(scr_d, (20 + pp) * SCRW, [[TC, 128], [1, TC]]),
                reads=(f"scro{2 * pp}", f"scro{2 * pp + 1}"),
                writes=(f"op{pp}",) + extra_w)

        def _womm(E):
            last = None
            for nb in range(2):
                for pp in range(FCH):
                    last = E.matmul(
                        P[nb][:, 0:512], o_pair[:, pp * TC:(pp + 1) * TC],
                        wo_sb[:, pp * D + nb * 512: pp * D + (nb + 1) * 512],
                        start=(pp == 0), stop=(pp == FCH - 1))
            return last

        group("tensor", reads=tuple(f"op{pp}" for pp in range(FCH)) +
              tuple(f"wo{k}" for k in range(FCH)),
              writes=("bank0", "bank1"), fn=_womm)

        def _resid(E, xa=xa):
            E.tensor_tensor(out=xa[:, 0:512], in0=xa[:, 0:512],
                            in1=P[0][:, 0:512], op=ALU.add)
            return E.tensor_tensor(out=xa[:, 512:1024], in0=xa[:, 512:1024],
                                   in1=P[1][:, 0:512], op=ALU.add)

        group("vector", reads=("bank0", "bank1", xb), writes=(xb,), fn=_resid)
        dma("gpsimd", ap3(scr_d, (40 + 8 * tci) * SCRW, [[1024, 128], [1, 1024]]),
            xa, reads=(xb,), writes=(f"spill{tci}",))

        if debug and tci == tap_tci:
            dma("gpsimd", taps["t_xp"][:, :], xa, reads=(xb,))
            dma("gpsimd", taps["t_h"][:, :], h_tm[:, :], reads=("h_tm",))

    ATTN_BUFS = (tuple(f"qkv{m}" for m in range(QKV_CH)) +
                 tuple(f"qp{m}" for m in range(FCH)) +
                 tuple(f"qf{h}" for h in range(HQ)) +
                 tuple(f"on{k}" for k in range(NQ)) +
                 tuple(f"op{p}" for p in range(FCH)) +
                 tuple(f"scro{h}" for h in range(HQ)) +
                 tuple(f"scrq{m}" for m in range(FCH)) +
                 ("qfmv", "rowscw", "wld0", "wld1", "h_fm", "h_fma", "h_tm",
                  "csc1", "csc2", "kaugw0", "kaugw1", "vpartsw0", "vpartsw1",
                  "dhat", "amatA", "amatB", "ufm", "ztb", "zfm",
                  "eaug", "ehat", "b_fm", "g_fmw", "gsp", "ale",
                  "expB", "ebpad", "lstageb", "dstage0", "mv2", "stat6",
                  "xslot0", "xslot1") +
                 tuple("wq" + str(k) for k in range(FCH)) +
                 tuple("wk" + str(k) for k in range(2)) +
                 tuple("wv" + str(k) for k in range(2)) +
                 tuple("waux" + str(k) for k in range(FCH)) +
                 tuple("wo" + str(k) for k in range(FCH)) +
                 tuple(f"kbar{k}" for k in range(NQ)) +
                 tuple(f"katm{k}" for k in range(NQ)))
    group("gpsimd", reads=(), writes=ATTN_BUFS + ("flip",),
          fn=lambda E: E.engine_nop())

    FFN_TC = main_tc if do_ffn else 0

    for wi, wd in enumerate((wr_d, wk2_d, wv2_d)):
        wbase = wi * FCH * D
        for dc in range(FCH):
            for cp in range(2):
                slot = (dc * 2 + cp) % 2
                nm = f"fw{wi}_{dc}_{cp}"
                dma("sync", wld[:, slot * 512:(slot + 1) * 512],
                    wd[dc * 128:(dc + 1) * 128, cp * 512:(cp + 1) * 512],
                    reads=(f"wld{slot}", "flip"), writes=(nm, f"wld{slot}"))

                def _tr(E, slot=slot):
                    last = None
                    for q in range(4):
                        last = E.transpose(
                            P[2][:, q * 128:(q + 1) * 128],
                            wld[:, slot * 512 + q * 128:
                                slot * 512 + (q + 1) * 128], i128[:, :])
                    return last

                group("tensor", reads=(nm, f"wld{slot}", "i128"),
                      writes=("bank2",), fn=_tr)

                def _ev(E, wi=wi, dc=dc, cp=cp, wbase=wbase):
                    last = None
                    for q in range(4):
                        cchunk = (cp * 512 + q * 128) // 128
                        dst = wfT[:, wbase + cchunk * D + dc * 128:
                                  wbase + cchunk * D + (dc + 1) * 128]
                        if wi < 2:
                            last = E.activation(
                                out=dst, in_=P[2][:, q * 128:(q + 1) * 128],
                                func=AF.Identity, bias=0.0,
                                scale=deriv[:, 16 + cchunk:17 + cchunk])
                        else:
                            last = E.copy(out=dst,
                                          in_=P[2][:, q * 128:(q + 1) * 128])
                    return last

                group("scalar", reads=("bank2", "deriv", "flip"),
                      writes=(f"wfT{wi}_{dc}_{cp}",), fn=_ev)

    wfT0 = tuple(f"wfT0_{dc}_{cp}" for dc in range(FCH) for cp in range(2))
    wfT1 = tuple(f"wfT1_{dc}_{cp}" for dc in range(FCH) for cp in range(2))
    wfT2 = tuple(f"wfT2_{dc}_{cp}" for dc in range(FCH) for cp in range(2))
    for cchunk in range(FCH if do_ffn else 0):
        def _gbf(E, cchunk=cchunk):
            dst = wfT[:, 2 * FCH * D + cchunk * D:
                      2 * FCH * D + (cchunk + 1) * D]
            return E.tensor_tensor(out=dst, in0=dst, in1=gb_rep[:, :],
                                   op=ALU.mult)

        group("vector", reads=wfT2 + GBREPS, writes=(f"wvT{cchunk}",),
              fn=_gbf)

    def _brmm(E):
        last = None
        for nb in range(2):
            for cchunk in range(FCH):
                last = E.matmul(
                    P[3 + nb][0:1, 0:512],
                    deriv[:, 40 + cchunk:41 + cchunk],
                    wfT[:, cchunk * D + nb * 512: cchunk * D + (nb + 1) * 512],
                    start=(cchunk == 0), stop=(cchunk == FCH - 1))
        return last

    if do_ffn:
        group("tensor", reads=wfT0 + ("deriv",), writes=("bank3", "bank4"),
              fn=_brmm)

    def _brcp(E):
        E.copy(out=rowsc[0:1, 0:512], in_=P[3][0:1, 0:512])
        return E.copy(out=rowsc[0:1, 512:1024], in_=P[4][0:1, 0:512])

    if do_ffn:
        group("scalar", reads=("bank3", "bank4", "rowscw"), writes=("rowscw",),
              fn=_brcp)
    for j in range(4 if do_ffn else 0):
        def _r1(E, j=j):
            return E.matmul(P[3][:, 0:256], onescol[:, 0:128],
                            rowsc[:, j * 256:(j + 1) * 256], start=True,
                            stop=True)

        group("tensor", reads=("rowscw", "onescol"), writes=("bank3",),
              fn=_r1)
        group("scalar", reads=("bank3",), writes=(f"brrep{j}",),
              fn=lambda E, j=j: E.copy(out=br_rep[:, j * 256:(j + 1) * 256],
                                       in_=P[3][:, 0:256]))
    BRREPS = ("brrep0", "brrep1", "brrep2", "brrep3")

    def _bkmm(E):
        last = None
        for nb in range(2):
            for cchunk in range(FCH):
                last = E.matmul(
                    P[3 + nb][0:1, 0:512],
                    deriv[:, 40 + cchunk:41 + cchunk],
                    wfT[:, FCH * D + cchunk * D + nb * 512:
                        FCH * D + cchunk * D + (nb + 1) * 512],
                    start=(cchunk == 0), stop=(cchunk == FCH - 1))
        return last

    if do_ffn:
        group("tensor", reads=wfT1 + ("deriv",), writes=("bank3", "bank4"),
              fn=_bkmm)

    def _bkcp(E):
        E.copy(out=rowsc[0:1, 0:512], in_=P[3][0:1, 0:512])
        return E.copy(out=rowsc[0:1, 512:1024], in_=P[4][0:1, 0:512])

    if do_ffn:
        group("scalar", reads=("bank3", "bank4", "rowscw"), writes=("rowscw",),
              fn=_bkcp)
        dma("gpsimd", scr_d[30:31, 0:1024], rowsc[0:1, 0:1024],
            reads=("rowscw",), writes=("scr30",))
        dma("gpsimd", biask2[:, :],
            ap3(scr_d, 30 * SCRW, [[1, 128], [128, FCH]]),
            reads=("scr30",), writes=("biask2",), slow=True)

    if debug and do_ffn:
        dma("gpsimd", taps["t_f7"][:, 0:D], wfT[:, 0:D],
            reads=("wfT0_0_0", "wfT0_0_1"))
        dma("gpsimd", taps["t_f7"][:, D:2 * D], wfT[:, FCH * D:FCH * D + D],
            reads=("wfT1_0_0", "wfT1_0_1"))
        dma("gpsimd", taps["t_f7"][:, 2 * D:3 * D],
            wfT[:, 2 * FCH * D:2 * FCH * D + D],
            reads=("wfT2_0_0", "wfT2_0_1", "wvT0"))
        dma("gpsimd", taps["t_f5"][:, 1040:1048], deriv[:, 16:24],
            reads=("deriv",))

    for tci in range(FFN_TC):
        slot = tci % 2
        xa = x_sb[:, slot * D:(slot + 1) * D]
        xb = f"xslot{slot}"
        dma("sync", xa,
            ap3(scr_d, (40 + 8 * tci) * SCRW, [[1024, 128], [1, 1024]]),
            reads=(xb, f"spill{tci}", "flip"), writes=(xb,))
        ln_to_h(xa, xb)

        def _h2halo(E, tci=tci):
            last = E.tensor_copy(
                out=ap3(h2_fm, 0, [[ps_of(h2_fm), 128], [TC + 1, FCH], [1, 1]]),
                in_=ap3(h2_fm, TC, [[ps_of(h2_fm), 128], [TC + 1, FCH], [1, 1]]))
            if tci == 1:
                E.drain()
                for f in range(FCH):
                    last = E.tensor_scalar(
                        out=h2_fm[:, f * (TC + 1):f * (TC + 1) + 1],
                        in0=h2_fm[:, f * (TC + 1):f * (TC + 1) + 1],
                        scalar1=flag[:, 0:1], scalar2=dscr[:, 52 + f:53 + f],
                        op0=ALU.mult, op1=ALU.add)
            return last

        group("gpsimd", reads=("flag", "h2f", "h2fa", "flip", "dscr"),
              writes=("h2f", "h2fa", "h2halo"), fn=_h2halo)
        h_transpose("h2fa", "h2f",
                    lambda f: h2_fm[:, f * (TC + 1) + 1: f * (TC + 1) + 1 + TC],
                    extra_a=("h2halo",))

        def _mix(E):
            E.drain()
            last = None
            for f in range(FCH):
                b0 = f * (TC + 1)
                E.tensor_tensor(out=mixscr[:, :],
                                in0=h2_fm[:, b0 + 1:b0 + 1 + TC],
                                in1=h2_fm[:, b0:b0 + TC], op=ALU.subtract)
                E.scalar_tensor_tensor(out=xr_fm[:, f * TC:(f + 1) * TC],
                                       in0=mixscr[:, :],
                                       scalar=murc[:, f:f + 1],
                                       in1=h2_fm[:, b0:b0 + TC],
                                       op0=ALU.mult, op1=ALU.add)
                last = E.scalar_tensor_tensor(
                    out=xk_fm[:, f * TC:(f + 1) * TC], in0=mixscr[:, :],
                    scalar=mukc[:, f:f + 1], in1=h2_fm[:, b0:b0 + TC],
                    op0=ALU.mult, op1=ALU.add)
            return last

        group("vector", reads=("h2f", "h2fa", "h2halo", "murc", "mukc",
                               "flip"),
              writes=("xrk", "mixscr"), fn=_mix)

        def _rmm(E):
            last = None
            for nb in range(2):
                for cchunk in range(FCH):
                    last = E.matmul(
                        P[nb][:, 0:512],
                        xr_fm[:, cchunk * TC:(cchunk + 1) * TC],
                        wfT[:, cchunk * D + nb * 512:
                            cchunk * D + (nb + 1) * 512],
                        start=(cchunk == 0), stop=(cchunk == FCH - 1))
            return last

        group("tensor", reads=("xrk",) + wfT0, writes=("bank0", "bank1"),
              fn=_rmm)

        def _rpre(E):
            E.tensor_tensor(out=r_tm[:, 0:512], in0=P[0][:, 0:512],
                            in1=br_rep[:, 0:512], op=ALU.add)
            return E.tensor_tensor(out=r_tm[:, 512:1024], in0=P[1][:, 0:512],
                                   in1=br_rep[:, 512:1024], op=ALU.add)

        group("vector", reads=("bank0", "bank1", "flip") + BRREPS,
              writes=(("r_tm",) + (GBREPS if tci == 0 else ())), fn=_rpre)
        group("scalar", reads=("r_tm",), writes=("r_tm",),
              fn=lambda E: E.activation(out=r_tm[:, :], in_=r_tm[:, :],
                                        func=AF.Sigmoid))

        for f in range(FCH):
            bank = 4 + (f % 2)

            def _kmm(E, f=f, bank=bank):
                last = None
                for cchunk in range(FCH):
                    last = E.matmul(
                        P[bank][:, 0:TC],
                        wfT[:, FCH * D + cchunk * D + f * 128:
                            FCH * D + cchunk * D + (f + 1) * 128],
                        xk_fm[:, cchunk * TC:(cchunk + 1) * TC],
                        start=(cchunk == 0), stop=(cchunk == FCH - 1))
                return last

            group("tensor", reads=("xrk",) + wfT1, writes=(f"bank{bank}",),
                  fn=_kmm)

            def _krelu(E, f=f, bank=bank):
                return E.activation(out=kact[:, f * TC:(f + 1) * TC],
                                    in_=P[bank][:, 0:TC], func=AF.Relu,
                                    bias=biask2[:, f:f + 1], scale=1.0)

            group("scalar", reads=(f"bank{bank}", "biask2", "flip"),
                  writes=(f"ka{f}",), fn=_krelu)

            def _ksq(E, f=f):
                return E.tensor_tensor(out=kact[:, f * TC:(f + 1) * TC],
                                       in0=kact[:, f * TC:(f + 1) * TC],
                                       in1=kact[:, f * TC:(f + 1) * TC],
                                       op=ALU.mult)

            group("vector", reads=(f"ka{f}",), writes=(f"ka{f}",), fn=_ksq)

        def _vmm(E):
            last = None
            for nb in range(2):
                for cchunk in range(FCH):
                    last = E.matmul(
                        P[6 + nb][:, 0:512],
                        kact[:, cchunk * TC:(cchunk + 1) * TC],
                        wfT[:, 2 * FCH * D + cchunk * D + nb * 512:
                            2 * FCH * D + cchunk * D + (nb + 1) * 512],
                        start=(cchunk == 0), stop=(cchunk == FCH - 1))
            return last

        group("tensor", reads=tuple(f"ka{f}" for f in range(FCH)) +
              tuple(f"wvT{cchunk}" for cchunk in range(FCH)),
              writes=("bank6", "bank7"), fn=_vmm)

        def _fin(E, xa=xa):
            E.tensor_tensor(out=xp_tm[:, 0:512], in0=r_tm[:, 0:512],
                            in1=P[6][:, 0:512], op=ALU.mult)
            E.tensor_tensor(out=xp_tm[:, 512:1024], in0=r_tm[:, 512:1024],
                            in1=P[7][:, 0:512], op=ALU.mult)
            E.tensor_tensor(out=xp_tm[:, 0:512], in0=xp_tm[:, 0:512],
                            in1=xa[:, 0:512], op=ALU.add)
            return E.tensor_tensor(out=xp_tm[:, 512:1024],
                                   in0=xp_tm[:, 512:1024],
                                   in1=xa[:, 512:1024], op=ALU.add)

        group("vector", reads=("bank6", "bank7", "r_tm", xb, "flip"),
              writes=("xout",), fn=_fin)
        if tci >= 1:
            dma("gpsimd", y_d[(tci - 1) * TC:tci * TC, :], xp_tm[:, :],
                reads=("xout",))
        if debug and tci == tap_tci:
            dma("gpsimd", taps["t_misc"][:, 0:1024], r_tm[:, 0:1024],
                reads=("r_tm",))
            dma("gpsimd", taps["t_f1"][:, :], h2_fm[:, :],
                reads=("h2f", "h2fa", "h2halo"))
            dma("gpsimd", taps["t_f2"][:, :], xr_fm[:, :], reads=("xrk",))
            dma("gpsimd", taps["t_f3"][:, :], xk_fm[:, :], reads=("xrk",))
            dma("gpsimd", taps["t_f4"][:, :], kact[:, :],
                reads=tuple(f"ka{f}" for f in range(FCH)))
            dma("gpsimd", taps["t_f5"][:, 0:1024], br_rep[:, :],
                reads=BRREPS)
            dma("gpsimd", taps["t_f5"][:, 1024:1032], biask2[:, :],
                reads=("biask2",))
            dma("gpsimd", taps["t_f6"][:, 0:D], wfT[:, 0:D],
                reads=("wfT0_0_0", "wfT0_0_1"))
            dma("gpsimd", taps["t_f6"][:, D:2 * D], wfT[:, FCH * D:FCH * D + D],
                reads=("wfT1_0_0", "wfT1_0_1"))
            dma("gpsimd", taps["t_f6"][:, 2 * D:3 * D],
                wfT[:, 2 * FCH * D:2 * FCH * D + D],
                reads=("wfT2_0_0", "wfT2_0_1", "wvT0"))

    ngps = dcount["gpsimd"]
    nsync = dcount["sync"]
    thunks["gpsimd"].append(
        lambda E, n=ngps: E.wait_ge(dsems["gpsimd"], 16 * n))
    thunks["sync"].append(
        lambda E, n=nsync: E.wait_ge(dsems["sync"], 16 * n))

    with nc.Block() as block:
        for eng in ENGINES:
            def run_all(E, eng=eng):
                for t in thunks[eng]:
                    t(E)

            getattr(block, eng)(run_all)

    for cm in reversed(cms):
        cm.__exit__(None, None, None)
    return nc


_NC_CACHE = {}


def _get_nc(debug=False):
    if debug not in _NC_CACHE:
        _NC_CACHE[debug] = build_program(debug)
    return _NC_CACHE[debug]


def make_in_maps(L0_attn_x, L0_attn_Wq, L0_attn_Wk, L0_attn_Wv, L0_attn_Wconv,
                 L0_attn_Wa, L0_attn_Walpha, L0_attn_Wb, L0_attn_A_log,
                 L0_attn_dt_bias, L0_attn_norm_w, L0_attn_Wo, L0_ffn_W_r,
                 L0_ffn_W_k, L0_ffn_W_v, L0_ffn_mu_r, L0_ffn_mu_k, L0_c,
                 L0_wmod):
    x = np.ascontiguousarray(np.asarray(L0_attn_x, dtype=np.float32))
    waux = np.zeros((D, AUXF), np.float32)
    waux[:, 0:16] = np.asarray(L0_attn_Wa, np.float32)
    waux[:, 32:48] = np.asarray(L0_attn_Walpha, np.float32)
    waux[:, 64:68] = np.asarray(L0_attn_Wb, np.float32)
    shared = {
        "wq": np.ascontiguousarray(np.asarray(L0_attn_Wq, np.float32)),
        "wk": np.ascontiguousarray(np.asarray(L0_attn_Wk, np.float32)),
        "wv": np.ascontiguousarray(np.asarray(L0_attn_Wv, np.float32)),
        "waux": waux,
        "wcv": np.ascontiguousarray(np.asarray(L0_attn_Wconv, np.float32)),
        "alog": np.asarray(L0_attn_A_log, np.float32).reshape(1, HQ),
        "dtb": np.asarray(L0_attn_dt_bias, np.float32).reshape(1, HQ),
        "nw": np.asarray(L0_attn_norm_w, np.float32).reshape(1, HD),
        "wo": np.ascontiguousarray(np.asarray(L0_attn_Wo, np.float32)),
        "wr": np.ascontiguousarray(np.asarray(L0_ffn_W_r, np.float32)),
        "wk2": np.ascontiguousarray(np.asarray(L0_ffn_W_k, np.float32)),
        "wv2": np.ascontiguousarray(np.asarray(L0_ffn_W_v, np.float32)),
        "mur": np.asarray(L0_ffn_mu_r, np.float32).reshape(1, D),
        "muk": np.asarray(L0_ffn_mu_k, np.float32).reshape(1, D),
        "wmod": np.ascontiguousarray(np.asarray(L0_wmod, np.float32)),
    }
    in_maps = []
    for core in range(8):
        b, half = core // 2, core % 2
        xb = x[b]
        if half == 0:
            mxv = np.concatenate(
                [np.zeros((TC, D), np.float32), xb[0:2048]], axis=0)
            fl = np.zeros((128, 1), np.float32)
        else:
            mxv = xb[2048 - TC:4096]
            fl = np.ones((128, 1), np.float32)
        im = dict(shared)
        im["px"] = np.ascontiguousarray(xb[0:PRE_TC * TC])
        im["mx"] = np.ascontiguousarray(mxv)
        im["cvec"] = np.asarray(L0_c, np.float32)[b:b + 1]
        im["flag"] = fl
        in_maps.append(im)
    return in_maps


def kernel(**inputs):
    nc = _get_nc(False)
    in_maps = make_in_maps(**inputs)
    res = run_bass_kernel_spmd(nc, in_maps, core_ids=list(range(8)))
    out = np.empty((B, T, D), np.float32)
    for core in range(8):
        b, half = core // 2, core % 2
        out[b, half * 2048:(half + 1) * 2048] = res.results[core]["y"]
    return out
